# version 03 (final)
# baseline (speedup 1.0000x reference)
import math
import numpy as np
import concourse.bass as bass
import concourse.mybir as mybir
from concourse.bass_utils import run_bass_kernel_spmd

F32 = mybir.dt.float32
BF16 = mybir.dt.bfloat16
U8 = mybir.dt.uint8
ALU = mybir.AluOpType
ACT = mybir.ActivationFunctionType
AX = mybir.AxisListType
AP = bass.AP

NT = 8
D = 2048
DC = 16
TOK = 1024
NEG = -3.0e38
C1 = 6.28125
C2 = 2.0 * math.pi - 6.28125
INV2PI = 1.0 / (2.0 * math.pi)

_STAGE = 99


class Eng:
    def __init__(self, eng, sem, step=1, autodrain=False):
        self.e = eng
        self.sem = sem
        self.step = step
        self.cnt = 0
        self.autodrain = autodrain
        self._waits = {}

    def wait(self, tok):
        if tok is None:
            return
        sem, val = tok
        if val <= 0:
            return
        k = id(sem)
        if self._waits.get(k, -1) >= val:
            return
        self.e.wait_ge(sem, val)
        self._waits[k] = val

    def waits(self, toks):
        for t in toks:
            self.wait(t)

    def inc(self, inst):
        inst.then_inc(self.sem, self.step)
        self.cnt += self.step
        if self.autodrain:
            self.e.drain()
        return (self.sem, self.cnt)

    def tok(self):
        return (self.sem, self.cnt)

    def fence(self):
        self.e.wait_ge(self.sem, self.cnt)
        self._waits[id(self.sem)] = self.cnt


def build_program(stage=None):
    if stage is None:
        stage = _STAGE
    nc = bass.Bass("TRN2", target_bir_lowering=False)

    xin = nc.dram_tensor("x", [TOK, D], F32, kind="ExternalInput").ap()
    rkin = nc.dram_tensor("rk", [1, 1], F32, kind="ExternalInput").ap()
    W = {}
    for nm, sh in [
        ("wq0", [D, D]), ("wk0", [D, D]), ("wv0", [D, D]), ("wo0", [D, D]),
        ("wf0", [D, D]), ("fb0", [1, D]),
        ("wq1", [D, D]), ("bq1", [1, D]), ("wk1", [D, 512]), ("bk1", [1, 512]),
        ("wv1", [D, 512]), ("bv1", [1, 512]), ("wo1", [D, D]), ("bo1", [1, D]),
        ("gw", [D, 8]), ("gb", [1, 8]),
        ("w1", [9, D, 1024]), ("w2", [9, 1024, D]), ("w3", [9, D, 1024]),
    ]:
        W[nm] = nc.dram_tensor(nm, sh, F32, kind="ExternalInput").ap()
    out = nc.dram_tensor("out", [TOK, D], F32, kind="ExternalOutput").ap()

    xpark = nc.dram_tensor("xpark", [TOK, D], F32).ap()
    agin = [nc.dram_tensor("agin0", [16384, 512], BF16).ap(),
            nc.dram_tensor("agin1", [4096, 512], BF16).ap()]
    agout = [nc.dram_tensor("agout0", [65536, 512], BF16).ap(),
             nc.dram_tensor("agout1", [16384, 512], BF16).ap()]
    atnh_d = nc.dram_tensor("atnh_d", [2048, 1024], BF16).ap()
    atnl_d = nc.dram_tensor("atnl_d", [2048, 1024], BF16).ap()

    ARENA = 188 * 1024
    arena_cm = nc.sbuf_tensor("arena", [128, ARENA], U8)
    arena = arena_cm.__enter__()
    arena_addr = nc.lookup_mloc(arena).addr
    _pcur = [131072]

    def place(name, shape, dtype, off=None):
        h = nc.virtual_sbuf_tensor(name, shape, dtype)
        per = 1
        for s in shape[1:]:
            per *= s
        per *= mybir.dt.size(dtype)
        if off is None:
            off = (_pcur[0] + 31) & ~31
            _pcur[0] = off + per
            assert _pcur[0] <= 148480, f"persistent overflow at {name}: {_pcur[0]}"
        else:
            assert off + per <= ARENA, f"arena overflow {name}: {off + per}"
        nc._tensor(h.name, h.shape, h.dtype, type="SB")
        mloc = nc.lookup_mloc(h)
        mloc.allocated = True
        mloc.addr = arena_addr + off
        return h

    A0, B0, U0 = 0, 65536, 148480
    xs = place("xs", [128, NT * D], F32, A0)
    qTh = place("qTh", [128, DC * 1024], BF16, A0)
    qTl = place("qTl", [128, DC * 1024], BF16, A0 + 32768)
    xnTh = place("xnTh", [128, DC * 1024], BF16, B0)
    xnTl = place("xnTl", [128, DC * 1024], BF16, B0 + 32768)
    kvbuf = []
    for p in range(2):
        kh = place(f"k{p}h", [128, 4096], BF16, B0 + 32768 * p)
        kl = place(f"k{p}l", [128, 4096], BF16, B0 + 32768 * p + 8192)
        vh = place(f"v{p}h", [128, 4096], BF16, B0 + 32768 * p + 16384)
        vl = place(f"v{p}l", [128, 4096], BF16, B0 + 32768 * p + 24576)
        kvbuf.append((kh, kl, vh, vl))
    atnh = place("atnh", [128, DC * 1024], BF16, B0)
    atnl = place("atnl", [128, DC * 1024], BF16, B0 + 32768)
    nTq = place("nTq", [128, DC * 1024], BF16, B0)
    lad1 = place("lad1", [128, 1024], F32, B0 + 32768)
    lad2 = place("lad2", [128, 1024], F32, B0 + 36960)
    lad3 = place("lad3", [128, 1024], F32, B0 + 41152)
    nbq = place("nbq", [128, 2048], F32, B0 + 45344)
    fbT = place("fbT", [128, 2048], F32, B0 + 53536)

    ident32 = place("ident32", [128, 128], F32)
    identbf = place("identbf", [128, 128], BF16)
    iotaP = place("iotaP", [128, 1], F32)
    iotaF = place("iotaF", [128, 128], F32)
    onesb = place("onesb", [128, 1], BF16)
    ones1f = place("ones1f", [1, 128], F32)
    onescol = place("onescol", [128, 1], F32)
    rkb = place("rkb", [128, 1], F32)
    rk128 = place("rk128", [128, 1], F32)
    tri = place("tri", [128, 128], BF16)
    maskA = place("maskA", [128, 4 * 256], BF16)
    maskB = place("maskB", [128, 4 * 256], BF16)
    cosT = place("cosT", [128, NT * 64], F32)
    sinT = place("sinT", [128, NT * 64], F32)
    invfb = place("invfb", [128, 64], F32)
    frow = place("frow", [1, 64], F32)
    angb = place("angb", [128, 64], F32)
    ang2 = place("ang2", [128, 64], F32)
    ladA = place("ladA", [128, 64], F32)
    ladB = place("ladB", [128, 64], F32)
    posc = place("posc", [128, 1], F32)
    stats = place("stats", [128, 24], F32)
    mv = place("mv", [128, 2], F32)
    rstd = place("rstd", [128, 1], F32)
    sm1 = place("sm1", [128, 1], F32)
    sm2 = place("sm2", [128, 1], F32)
    sm3 = place("sm3", [128, 1], F32)
    gwh = place("gwh", [128, 128], BF16)
    gwl = place("gwl", [128, 128], BF16)
    gw32 = place("gw32", [128, 128], F32)
    gbB = place("gbB", [128, 8], F32)
    dwbuf = place("dwbuf", [128, 64], F32)
    salq = place("salq", [128, 8], F32)
    g8a = place("g8a", [128, 8], F32)
    g8b = place("g8b", [128, 8], F32)
    g8c = place("g8c", [128, 8], F32)
    alv = place("alv", [1, 8], F32)
    accw = place("accw", [128, 1], F32)
    thrp = place("thrp", [128, 1], F32)
    thrn = place("thrn", [128, 1], F32)
    alb = place("alb", [128, 1], F32)
    srow = place("srow", [1, 256], F32)

    qtok = [place(f"qtok{i}", [128, 512], F32, U0 + 2048 * i) for i in range(2)]
    qrot = [place(f"qrot{i}", [128, 512], F32, U0 + 4096 + 2048 * i) for i in range(2)]
    tmpA = place("tmpA", [128, 256], F32, U0 + 8192)
    tmpB = place("tmpB", [128, 256], F32, U0 + 9216)
    biasr32 = place("biasr32", [1, 512], F32, U0 + 8192)
    wf32 = [place(f"wf32_{i}", [128, 1024], F32, U0 + 10240 + 4096 * i) for i in range(2)]
    wbh = [place(f"wbh{i}", [128, 512], BF16, U0 + 18432 + 1024 * i) for i in range(2)]
    wbl = [place(f"wbl{i}", [128, 512], BF16, U0 + 20480 + 1024 * i) for i in range(2)]
    kstgh = place("kstgh", [128, 4096], BF16, U0 + 22528)
    kstgl = place("kstgl", [128, 4096], BF16, U0 + 30720)
    vstgh = [place(f"vstgh{i}", [128, 512], BF16, U0 + 22528 + 1024 * i) for i in range(2)]
    vstgl = [place(f"vstgl{i}", [128, 512], BF16, U0 + 24576 + 1024 * i) for i in range(2)]
    Pf = [place(f"Pf{i}", [128, 256], F32, U0 + 2048 * i) for i in range(2)]
    Psp = [place(f"Psp{i}", [128, 512], BF16, U0 + 4096 + 1024 * i) for i in range(2)]
    attnF = place("attnF", [128, 256], F32, U0 + 8192)
    srow2 = place("srow2", [1, 512], F32, U0 + 8192)
    recipB = place("recipB", [128, 256], F32, U0 + 9216)
    athb = place("athb", [128, 1024], BF16, U0 + 10240)
    atlb = place("atlb", [128, 1024], BF16, U0 + 12288)
    ftt = [place(f"ftt{i}", [128, 512], BF16, U0 + 18432 + 1024 * i) for i in range(2)]
    ft1 = place("ft1", [128, 512], F32, U0 + 4096)
    ft2 = place("ft2", [128, 512], F32, U0 + 6144)
    xnb = place("xnb", [128, 2048], F32, U0 + 26624)
    s1b = place("s1b", [128, 2048], F32, U0)
    hTh = place("hTh", [128, 4096], BF16, U0 + 22528)
    hTl = place("hTl", [128, 4096], BF16, U0 + 30720)
    hFt = [place(f"hFt{i}", [128, 512], F32, U0 + 38912 + 2048 * i) for i in range(2)]

    pb_cms = [nc.psum_tensor(f"pb{i}", [128, 512], F32) for i in range(8)]
    pb = [cm.__enter__() for cm in pb_cms]
    bank_free = [None] * 8

    sem_cms = [nc.semaphore(n) for n in
               ["svT", "svV", "svS", "svG", "sdSY", "sdGP", "sCC",
                "sdW0", "sdW1", "sdV0", "sdV1", "sdS0", "sdS1"]]
    sems = [cm.__enter__() for cm in sem_cms]
    (svT, svV, svS, svG, sdSY, sdGP, sCC,
     sdW0, sdW1, sdV0, sdV1, sdS0, sdS1) = sems

    blk_cm = nc.Block()
    blk_cm.__enter__()

    T = Eng(nc.tensor, svT)
    V = Eng(nc.vector, svV, autodrain=True)
    S = Eng(nc.scalar, svS, autodrain=True)
    G = Eng(nc.gpsimd, svG, autodrain=True)
    SY = Eng(nc.sync, sdSY, step=16)
    GD = Eng(nc.sync, sdGP, step=16)
    CC = Eng(nc.gpsimd, sCC)
    SYw = [Eng(nc.sync, sdW0, step=16), Eng(nc.sync, sdW1, step=16)]
    GDv = [Eng(nc.sync, sdV0, step=16), Eng(nc.sync, sdV1, step=16)]
    GDs = [Eng(nc.sync, sdS0, step=16), Eng(nc.sync, sdS1, step=16)]
    ENGS = [T, V, S, G, SY]
    ALLQ = [GD] + SYw + GDv + GDs

    def join():
        SY.fence()
        for q in ALLQ:
            q.fence()
        toks = [e.tok() for e in ENGS] + [q.tok() for q in ALLQ]
        for e in ENGS:
            e.waits(toks)

    def acq(i):
        T.wait(bank_free[i])

    for i in range(NT):
        SY.inc(nc.sync.dma_start(out=xs[:, D * i:D * (i + 1)],
                                 in_=xin[128 * i:128 * (i + 1), :]))
    SY.inc(nc.sync.dma_start(out=alv[0:1, 0:1], in_=rkin[0:1, 0:1]))
    SY.inc(nc.sync.dma_start(
        out=AP(gw32, 0, [[128, 128], [8, 16], [1, 8]]),
        in_=AP(W["gw"].tensor, 0, [[8, 128], [1024, 16], [1, 8]])))
    SY.inc(nc.sync.dma_start(out=gbB[0:1, 0:8], in_=W["gb"][0:1, 0:8]))
    tok_in = SY.tok()
    SY.fence()

    G.inc(nc.gpsimd.iota(iotaP[:], pattern=[[0, 1]], base=0, channel_multiplier=1,
                         allow_small_or_imprecise_dtypes=True))
    G.inc(nc.gpsimd.iota(iotaF[:], pattern=[[1, 128]], base=0, channel_multiplier=0,
                         allow_small_or_imprecise_dtypes=True))
    tok_iota = G.tok()

    V.waits([tok_iota, tok_in])
    V.inc(nc.vector.memset(onesb[:], 1.0))
    V.inc(nc.vector.memset(ones1f[:], 1.0))
    V.inc(nc.vector.memset(onescol[:], 1.0))
    V.inc(nc.vector.tensor_scalar(ident32[:], iotaF[:], iotaP[:, 0:1], None, ALU.is_equal))
    V.inc(nc.vector.tensor_scalar(identbf[:], iotaF[:], iotaP[:, 0:1], None, ALU.is_equal))
    V.inc(nc.vector.tensor_scalar(tri[:], iotaF[:], iotaP[:, 0:1], None, ALU.is_ge))
    tok_id = V.tok()
    T.wait(tok_id)
    tb = T.inc(nc.tensor.matmul(pb[0][0:128, 0:1], ones1f[:], alv[0:1, 0:1],
                                start=True, stop=True))
    V.wait(tb)
    V.inc(nc.vector.tensor_copy(rkb[:], pb[0][0:128, 0:1]))
    V.inc(nc.vector.tensor_scalar(rk128[:], rkb[:], 128.0, None, ALU.mult))
    bank_free[0] = V.tok()
    for jr in range(4):
        V.inc(nc.vector.tensor_scalar(sm1[:], rkb[:], float(jr), None, ALU.is_equal))
        V.inc(nc.vector.tensor_scalar(sm2[:], rkb[:], float(jr), None, ALU.is_gt))
        sm2b = AP(sm2, 0, [[1, 128], [0, 128]])
        V.inc(nc.vector.scalar_tensor_tensor(
            maskA[:, 256 * jr:256 * jr + 128], tri[:], sm1[:, 0:1], sm2b,
            ALU.mult, ALU.add))
        V.inc(nc.vector.memset(maskA[:, 256 * jr + 128:256 * (jr + 1)], 1.0))
        V.inc(nc.vector.memset(maskB[:, 256 * jr:256 * jr + 128], 0.0))
        V.inc(nc.vector.scalar_tensor_tensor(
            maskB[:, 256 * jr + 128:256 * (jr + 1)], tri[:], sm1[:, 0:1], sm2b,
            ALU.mult, ALU.add))
    V.inc(nc.vector.tensor_copy(gwh[:], gw32[:]))
    V.inc(nc.vector.scalar_tensor_tensor(gwl[:], gw32[:], 1.0, gwh[:],
                                         ALU.mult, ALU.subtract))
    T.wait(V.tok())
    acq(1)
    tb = T.inc(nc.tensor.matmul(pb[1][0:128, 0:8], ones1f[:], gbB[0:1, 0:8],
                                start=True, stop=True))
    V.wait(tb)
    V.inc(nc.vector.tensor_copy(gbB[:, 0:8], pb[1][0:128, 0:8]))
    bank_free[1] = V.tok()
    join()

    wsplit_free = [None, None]
    qtok_free = [None, None]
    vstg_free = [None, None]
    Pf_free = [None, None]
    Psp_free = [None, None]

    def stream_w(tensor_h, elem_off, ncols, par):
        SYw[par].wait(wsplit_free[par])
        td = SYw[par].inc(nc.sync.dma_start(
            out=wf32[par][:, 0:ncols],
            in_=AP(tensor_h, elem_off, [[tensor_h.shape[-1], 128], [1, ncols]])))
        S.wait(td)
        th = S.inc(nc.scalar.copy(wbh[par][:, 0:ncols], wf32[par][:, 0:ncols]))
        V.wait(th)
        return V.inc(nc.vector.scalar_tensor_tensor(
            wbl[par][:, 0:ncols], wf32[par][:, 0:ncols], 1.0,
            wbh[par][:, 0:ncols], ALU.mult, ALU.subtract))

    def mm3(bank_ap, lh, ll, rh, rl, start, stop):
        T.inc(nc.tensor.matmul(bank_ap, lh, rh, start=start, stop=False))
        T.inc(nc.tensor.matmul(bank_ap, lh, rl, start=False, stop=False))
        return T.inc(nc.tensor.matmul(bank_ap, ll, rh, start=False, stop=stop))

    def ln_tile(i):
        for c in range(4):
            V.inc(nc.vector.bn_stats(stats[:, 6 * c:6 * c + 6],
                                     xs[:, D * i + 512 * c: D * i + 512 * (c + 1)]))
        V.inc(nc.vector.bn_aggr(mv[:], stats[:]))
        tv = V.inc(nc.vector.tensor_scalar(rstd[:], mv[:, 1:2], 1e-5, None, ALU.add))
        S.wait(tv)
        S.inc(nc.scalar.activation(rstd[:], rstd[:], ACT.Ln))
        ts2 = S.inc(nc.scalar.activation(rstd[:], rstd[:], ACT.Exp, scale=-0.5))
        V.wait(ts2)
        return V.inc(nc.vector.tensor_scalar(
            xnb[:], xs[:, D * i:D * (i + 1)], mv[:, 0:1], rstd[:, 0:1],
            ALU.subtract, ALU.mult))

    def tsplit(src, scol, i, dsth, dstl, cbase, nck, tok_src, banks, guard=None):
        for cc in range(nck):
            w = banks[cc % len(banks)]
            acq(w)
            T.wait(tok_src)
            tt = T.inc(nc.tensor.transpose(
                pb[w][0:128, 0:128], src[:, scol + 128 * cc: scol + 128 * (cc + 1)],
                ident32[:]))
            c = cbase + cc
            dh_ = dsth[:, 1024 * c + 128 * i: 1024 * c + 128 * (i + 1)]
            dl_ = dstl[:, 1024 * c + 128 * i: 1024 * c + 128 * (i + 1)]
            S.wait(tt)
            if guard is not None:
                S.wait(guard)
            th = S.inc(nc.scalar.copy(dh_, pb[w][0:128, 0:128]))
            V.wait(th)
            if guard is not None:
                V.wait(guard)
            tl = V.inc(nc.vector.scalar_tensor_tensor(
                dl_, pb[w][0:128, 0:128], 1.0, dh_, ALU.mult, ALU.subtract))
            bank_free[w] = tl

    def rope_apply(par, i, mode):
        if mode == 2:
            return V.inc(nc.vector.tensor_copy(qrot[par][:], qtok[par][:]))
        if mode == 0:
            cv = AP(cosT, 64 * i, [[NT * 64, 128], [0, 4], [1, 64]])
            sv = AP(sinT, 64 * i, [[NT * 64, 128], [0, 4], [1, 64]])
            qe = AP(qtok[par], 0, [[512, 128], [128, 4], [2, 64]])
            qo = AP(qtok[par], 1, [[512, 128], [128, 4], [2, 64]])
            rE = AP(qrot[par], 0, [[512, 128], [128, 4], [1, 64]])
            rO = AP(qrot[par], 64, [[512, 128], [128, 4], [1, 64]])
            V.inc(nc.vector.tensor_tensor(tmpA[:], qe, cv, ALU.mult))
            V.inc(nc.vector.tensor_tensor(tmpB[:], qo, sv, ALU.mult))
            V.inc(nc.vector.tensor_sub(rE, tmpA[:], tmpB[:]))
            V.inc(nc.vector.tensor_tensor(tmpA[:], qe, sv, ALU.mult))
            V.inc(nc.vector.tensor_tensor(tmpB[:], qo, cv, ALU.mult))
            return V.inc(nc.vector.tensor_add(rO, tmpA[:], tmpB[:]))
        cv = AP(cosT, 64 * i, [[NT * 64, 128], [0, 8], [1, 32]])
        sv = AP(sinT, 64 * i, [[NT * 64, 128], [0, 8], [1, 32]])
        qlo = AP(qtok[par], 0, [[512, 128], [64, 8], [1, 32]])
        qhi = AP(qtok[par], 32, [[512, 128], [64, 8], [1, 32]])
        rlo = AP(qrot[par], 0, [[512, 128], [64, 8], [1, 32]])
        rhi = AP(qrot[par], 32, [[512, 128], [64, 8], [1, 32]])
        V.inc(nc.vector.tensor_tensor(tmpA[:], qlo, cv, ALU.mult))
        V.inc(nc.vector.tensor_tensor(tmpB[:], qhi, sv, ALU.mult))
        V.inc(nc.vector.tensor_sub(rlo, tmpA[:], tmpB[:]))
        V.inc(nc.vector.tensor_tensor(tmpA[:], qhi, cv, ALU.mult))
        V.inc(nc.vector.tensor_tensor(tmpB[:], qlo, sv, ALU.mult))
        return V.inc(nc.vector.tensor_add(rhi, tmpA[:], tmpB[:]))

    def rope_tables(L):
        nf = 64 if L == 0 else 32
        scale = -2.0 * math.log(10000.0) / (128.0 if L == 0 else 64.0)
        tg = G.inc(nc.gpsimd.iota(frow[:], pattern=[[1, 64]], base=0,
                                  channel_multiplier=0,
                                  allow_small_or_imprecise_dtypes=True))
        V.wait(tg)
        tv = V.inc(nc.vector.tensor_scalar(frow[0:1, 0:nf], frow[0:1, 0:nf],
                                           scale, None, ALU.mult))
        S.wait(tv)
        ts_ = S.inc(nc.scalar.activation(frow[0:1, 0:nf], frow[0:1, 0:nf], ACT.Exp))
        T.wait(ts_)
        acq(1)
        tb2 = T.inc(nc.tensor.matmul(pb[1][0:128, 0:nf], ones1f[:],
                                     frow[0:1, 0:nf], start=True, stop=True))
        V.wait(tb2)
        V.inc(nc.vector.tensor_copy(invfb[:, 0:nf], pb[1][0:128, 0:nf]))
        bank_free[1] = V.tok()

        def red_sin(src_ap, dst_ap):
            V.inc(nc.vector.tensor_scalar(ladA[:, 0:nf], src_ap, INV2PI, 0.5,
                                          ALU.mult, ALU.add))
            V.inc(nc.vector.tensor_copy(ladB[:, 0:nf], ladA[:, 0:nf]))
            for b in [512.0, 256.0, 128.0, 64.0, 32.0, 16.0, 8.0, 4.0, 2.0, 1.0]:
                V.inc(nc.vector.tensor_scalar(ang2[:, 0:nf], ladB[:, 0:nf], b,
                                              None, ALU.is_ge))
                V.inc(nc.vector.scalar_tensor_tensor(
                    ladB[:, 0:nf], ang2[:, 0:nf], -b, ladB[:, 0:nf],
                    ALU.mult, ALU.add))
            V.inc(nc.vector.tensor_sub(ladA[:, 0:nf], ladA[:, 0:nf], ladB[:, 0:nf]))
            V.inc(nc.vector.scalar_tensor_tensor(ang2[:, 0:nf], ladA[:, 0:nf], -C1,
                                                 src_ap, ALU.mult, ALU.add))
            tv2 = V.inc(nc.vector.scalar_tensor_tensor(
                ang2[:, 0:nf], ladA[:, 0:nf], -C2, ang2[:, 0:nf], ALU.mult, ALU.add))
            S.wait(tv2)
            return S.inc(nc.scalar.activation(dst_ap, ang2[:, 0:nf], ACT.Sin))

        for i in range(NT):
            base = 512 * i + (1 if L == 1 else 0)
            V.inc(nc.vector.scalar_tensor_tensor(posc[:], iotaP[:], float(base),
                                                 rk128[:], ALU.add, ALU.add))
            V.inc(nc.vector.tensor_scalar(angb[:, 0:nf], invfb[:, 0:nf],
                                          posc[:, 0:1], None, ALU.mult))
            tok_s = red_sin(angb[:, 0:nf], sinT[:, 64 * i:64 * i + nf])
            V.wait(tok_s)
            V.inc(nc.vector.tensor_scalar(angb[:, 0:nf], angb[:, 0:nf],
                                          math.pi / 2.0, None, ALU.add))
            tok_c = red_sin(angb[:, 0:nf], cosT[:, 64 * i:64 * i + nf])
            V.wait(tok_c)

    def bias_row(dram_row, ob):
        SY.wait(V.tok())
        return SY.inc(nc.sync.dma_start(
            out=biasr32[0:1, 0:512],
            in_=dram_row[0:1, 512 * ob:512 * (ob + 1)]))

    def proj(L, wdram, ncols, bias_dram, rope_mode, sink):
        nob = ncols // 512
        for ob in range(nob):
            tok_bias = bias_row(bias_dram, ob) if bias_dram is not None else None
            for ic in range(DC):
                par = ic % 2
                tw = stream_w(wdram.tensor, 128 * ic * ncols + 512 * ob, 512, par)
                T.wait(tw)
                for t in range(NT):
                    if ic == 0:
                        acq(t)
                        if tok_bias is not None:
                            T.wait(tok_bias)
                            T.inc(nc.tensor.matmul(pb[t][0:128, 0:512], ones1f[:],
                                                   biasr32[0:1, 0:512],
                                                   start=True, stop=False))
                    st0 = (ic == 0) and (tok_bias is None)
                    mm3(pb[t][0:128, 0:512],
                        xnTh[:, 1024 * ic + 128 * t: 1024 * ic + 128 * (t + 1)],
                        xnTl[:, 1024 * ic + 128 * t: 1024 * ic + 128 * (t + 1)],
                        wbh[par][:, 0:512], wbl[par][:, 0:512],
                        st0, ic == DC - 1)
                wsplit_free[par] = T.tok()
            tmm = T.tok()

            def evac(t):
                par = t % 2
                V.waits([tmm, qtok_free[par]])
                tv = V.inc(nc.vector.tensor_copy(qtok[par][:], pb[t][0:128, 0:512]))
                bank_free[t] = tv

            evac(0)
            for t in range(NT):
                par = t % 2
                if t + 1 < NT:
                    evac(t + 1)
                tok_r = rope_apply(par, t, rope_mode)
                qtok_free[par] = tok_r
                sink(ob, t, par, tok_r)

    def q_sink(ob, t, par, tok_r):
        tsplit(qrot[par], 0, t, qTh, qTl, 4 * ob, 4, tok_r,
               banks=[t, (t + 1) % NT])
        qtok_free[par] = V.tok()

    def make_k_sink(L, kt_rows):
        ag = agin[L]
        state = {"kstg_free": None}

        def k_sink(ob, t, par, tok_r):
            tsplit(qrot[par], 0, t, kstgh, kstgl, 0, 4, tok_r,
                   banks=[t, (t + 1) % NT], guard=state["kstg_free"] if t == 0 else None)
            qtok_free[par] = V.tok()
            if t == NT - 1:
                tok_sp = V.tok()
                for cc in range(4):
                    cglob = 4 * ob + cc
                    for stg, roff in [(kstgh, 0), (kstgl, kt_rows)]:
                        GD.wait(tok_sp)
                        GD.inc(nc.sync.dma_start(
                            out=AP(ag.tensor, (roff + 256 * cglob) * 512,
                                   [[1024, 128], [1, 1024]]),
                            in_=stg[:, 1024 * cc:1024 * (cc + 1)]))
                state["kstg_free"] = GD.tok()

        return k_sink

    def make_v_sink(L, v_base, nhd, dh):
        ag = agin[L]
        nv = nhd * dh * TOK // 512

        def v_sink(ob, t, par, tok_r):
            sp = t % 2
            S.waits([tok_r, vstg_free[sp]])
            th = S.inc(nc.scalar.copy(vstgh[sp][:], qrot[par][:]))
            V.waits([th, vstg_free[sp]])
            tl = V.inc(nc.vector.scalar_tensor_tensor(
                vstgl[sp][:], qrot[par][:], 1.0, vstgh[sp][:],
                ALU.mult, ALU.subtract))
            qtok_free[par] = tl
            hpo = 512 // dh
            h0 = ob * hpo
            for stg, roff in [(vstgh[sp], v_base), (vstgl[sp], v_base + nv)]:
                GDv[sp].wait(tl)
                GDv[sp].inc(nc.sync.dma_start(
                    out=AP(ag.tensor, roff * 512 + h0 * TOK * dh + 128 * t * dh,
                           [[dh, 128], [TOK * dh, hpo], [1, dh]]),
                    in_=AP(stg, 0, [[512, 128], [dh, hpo], [1, dh]])))
            vstg_free[sp] = GDv[sp].tok()

        return v_sink

    def stage_kv(L, h, par, nheads, dh, kt_rows, v_base, after_tok):
        kh, kl, vh, vl = kvbuf[par]
        ag = agout[L]
        seg = 16384 if L == 0 else 4096
        nv = nheads * dh * TOK // 512
        Q = GDs[par]
        Q.wait(after_tok)
        Q.wait(CC.tok())
        for j in range(4):
            if L == 0:
                base = (seg * j + 256 * h) * 512
                for dst, off in [(kh, 0), (kl, kt_rows * 512)]:
                    Q.inc(nc.sync.dma_start(
                        out=dst[:, 1024 * j:1024 * (j + 1)],
                        in_=AP(ag.tensor, base + off, [[1024, 128], [1, 1024]])))
            else:
                base = (seg * j + 128 * h) * 512
                for dst, off in [(kh, 0), (kl, kt_rows * 512)]:
                    Q.inc(nc.sync.dma_start(
                        out=dst[0:64, 1024 * j:1024 * (j + 1)],
                        in_=AP(ag.tensor, base + off, [[1024, 64], [1, 1024]])))
                    Q.inc(nc.sync.dma_start(
                        out=dst[64:128, 1024 * j:1024 * (j + 1)],
                        in_=AP(ag.tensor, base + off, [[1024, 64], [1, 1024]])))
            vseg = (seg * j + v_base) * 512
            for dst, off in [(vh, 0), (vl, nv * 512)]:
                Q.inc(nc.sync.dma_start(
                    out=AP(dst, j * 8 * dh, [[4096, 128], [dh, 8], [1, dh]]),
                    in_=AP(ag.tensor, vseg + off + h * TOK * dh,
                           [[dh, 128], [128 * dh, 8], [1, dh]])))
        return Q.tok()

    def attention(L, nheads, dh, kt_rows, v_base, qh_per_kv):
        scale = 1.0 / math.sqrt(dh)
        kv_cons = [None, None]
        tok_st = [None, None]
        tok_st[0] = stage_kv(L, 0, 0, nheads, dh, kt_rows, v_base, None)
        for h in range(nheads):
            par = h % 2
            if h + 1 < nheads:
                tok_st[1 - par] = stage_kv(L, h + 1, 1 - par, nheads, dh,
                                           kt_rows, v_base, kv_cons[1 - par])
            kh, kl, vh, vl = kvbuf[par]
            for qh in range(qh_per_kv):
                if L == 0:
                    qchunk, qbase, arow = h, 0, 128 * h
                else:
                    qchunk, qbase = 2 * h + qh // 2, 64 * (qh % 2)
                    arow = 256 * h + 64 * qh
                for j in range(4):
                    acq(2)
                    acq(3)
                    nch = 8 * j + 8
                    chunks = [(k, jr) for k in range(2 * j + 2) for jr in range(4)]
                    pend = {}

                    def emit_st(cidx):
                        k, jr = chunks[cidx]
                        w = cidx % 2
                        ccol = 1024 * jr + 128 * k
                        acq(4 + w)
                        T.wait(tok_st[par])
                        qs_h = qTh[qbase:qbase + dh,
                                   1024 * qchunk + 256 * j:1024 * qchunk + 256 * (j + 1)]
                        qs_l = qTl[qbase:qbase + dh,
                                   1024 * qchunk + 256 * j:1024 * qchunk + 256 * (j + 1)]
                        kk_h = kh[qbase:qbase + dh, ccol:ccol + 128]
                        kk_l = kl[qbase:qbase + dh, ccol:ccol + 128]
                        T.inc(nc.tensor.matmul(pb[4 + w][0:128, 0:256], kk_h, qs_h,
                                               start=True, stop=False))
                        T.inc(nc.tensor.matmul(pb[4 + w][0:128, 0:256], kk_h, qs_l,
                                               start=False, stop=False))
                        tS = T.inc(nc.tensor.matmul(pb[4 + w][0:128, 0:256],
                                                    kk_l, qs_h,
                                                    start=False, stop=True))
                        S.waits([tS, Pf_free[w]])
                        tE = S.inc(nc.scalar.activation(
                            Pf[w][:], pb[4 + w][0:128, 0:256], ACT.Exp, scale=scale))
                        bank_free[4 + w] = tE
                        if k >= 2 * j:
                            mk = maskA if k == 2 * j else maskB
                            V.waits([tE, Pf_free[w]])
                            tE = V.inc(nc.vector.tensor_tensor(
                                Pf[w][:], Pf[w][:],
                                mk[:, 256 * jr:256 * (jr + 1)], ALU.mult))
                        S.waits([tE, Psp_free[w]])
                        tH = S.inc(nc.scalar.copy(Psp[w][:, 0:256], Pf[w][:]))
                        V.wait(tH)
                        tL = V.inc(nc.vector.scalar_tensor_tensor(
                            Psp[w][:, 256:512], Pf[w][:], 1.0, Psp[w][:, 0:256],
                            ALU.mult, ALU.subtract))
                        Pf_free[w] = tL
                        pend[cidx] = (tH, tL)

                    def emit_av(cidx):
                        k, jr = chunks[cidx]
                        w = cidx % 2
                        tH, tL = pend.pop(cidx)
                        T.waits([tH, tL])
                        vcol = (8 * jr + k) * dh
                        vv_h = vh[:, vcol:vcol + dh]
                        vv_l = vl[:, vcol:vcol + dh]
                        first = cidx == 0
                        last = cidx == nch - 1
                        T.inc(nc.tensor.matmul(pb[2][0:dh, 0:256], vv_h,
                                               Psp[w][:, 0:256],
                                               start=first, stop=False))
                        T.inc(nc.tensor.matmul(pb[2][0:dh, 0:256], vv_h,
                                               Psp[w][:, 256:512],
                                               start=False, stop=False))
                        T.inc(nc.tensor.matmul(pb[2][0:dh, 0:256], vv_l,
                                               Psp[w][:, 0:256],
                                               start=False, stop=last))
                        tsum = T.inc(nc.tensor.matmul(pb[3][0:1, 0:512], onesb[:],
                                                      Psp[w][:, 0:512],
                                                      start=first, stop=last))
                        Psp_free[w] = tsum

                    for cidx in range(nch):
                        emit_st(cidx)
                        if cidx > 0:
                            emit_av(cidx - 1)
                    emit_av(nch - 1)
                    tAV = T.tok()
                    V.wait(tAV)
                    V.inc(nc.vector.tensor_copy(srow2[0:1, 0:512], pb[3][0:1, 0:512]))
                    V.inc(nc.vector.tensor_add(srow[0:1, 0:256], srow2[0:1, 0:256],
                                               srow2[0:1, 256:512]))
                    tR = V.inc(nc.vector.reciprocal(srow[0:1, 0:256], srow[0:1, 0:256]))
                    bank_free[3] = tR
                    acq(6)
                    T.wait(tR)
                    tB = T.inc(nc.tensor.matmul(pb[6][0:128, 0:256], ones1f[:],
                                                srow[0:1, 0:256], start=True, stop=True))
                    V.wait(tB)
                    V.inc(nc.vector.tensor_copy(recipB[:], pb[6][0:128, 0:256]))
                    bank_free[6] = V.tok()
                    tF = V.inc(nc.vector.tensor_tensor(attnF[0:dh, 0:256],
                                                       pb[2][0:dh, 0:256],
                                                       recipB[0:dh, 0:256], ALU.mult))
                    bank_free[2] = tF
                    S.wait(tF)
                    tAH = S.inc(nc.scalar.copy(athb[0:dh, 256 * j:256 * (j + 1)],
                                               attnF[0:dh, 0:256]))
                    V.wait(tAH)
                    V.inc(nc.vector.scalar_tensor_tensor(
                        atlb[0:dh, 256 * j:256 * (j + 1)], attnF[0:dh, 0:256], 1.0,
                        athb[0:dh, 256 * j:256 * (j + 1)], ALU.mult, ALU.subtract))
                tsp = V.tok()
                SY.wait(tsp)
                SY.inc(nc.sync.dma_start(out=atnh_d[arow:arow + dh, :],
                                         in_=athb[0:dh, 0:1024]))
                tW = SY.inc(nc.sync.dma_start(out=atnl_d[arow:arow + dh, :],
                                              in_=atlb[0:dh, 0:1024]))
                SY.fence()
                V.wait(tW)
                S.wait(tW)
            kv_cons[par] = T.tok()

    def wo_phase(L):
        for i in range(NT):
            SY.inc(nc.sync.dma_start(out=xs[:, D * i:D * (i + 1)],
                                     in_=xpark[128 * i:128 * (i + 1), :]))
        SY.inc(nc.sync.dma_start(
            out=AP(atnh, 0, [[16384, 128], [1024, 16], [1, 1024]]),
            in_=AP(atnh_d.tensor, 0, [[1024, 128], [131072, 16], [1, 1024]])))
        SY.inc(nc.sync.dma_start(
            out=AP(atnl, 0, [[16384, 128], [1024, 16], [1, 1024]]),
            in_=AP(atnl_d.tensor, 0, [[1024, 128], [131072, 16], [1, 1024]])))
        tok_at = SY.tok()
        SY.fence()
        wo = W["wo1"] if L else W["wo0"]
        bo = W["bo1"] if L else None
        for ob in range(4):
            tok_bias = bias_row(bo, ob) if bo is not None else None
            for ic in range(DC):
                par = ic % 2
                tw = stream_w(wo.tensor, 128 * ic * D + 512 * ob, 512, par)
                T.waits([tw, tok_at])
                for t in range(NT):
                    if ic == 0:
                        acq(t)
                        if tok_bias is not None:
                            T.wait(tok_bias)
                            T.inc(nc.tensor.matmul(pb[t][0:128, 0:512], ones1f[:],
                                                   biasr32[0:1, 0:512],
                                                   start=True, stop=False))
                    st0 = (ic == 0) and (tok_bias is None)
                    mm3(pb[t][0:128, 0:512],
                        atnh[:, 1024 * ic + 128 * t: 1024 * ic + 128 * (t + 1)],
                        atnl[:, 1024 * ic + 128 * t: 1024 * ic + 128 * (t + 1)],
                        wbh[par][:, 0:512], wbl[par][:, 0:512],
                        st0, ic == DC - 1)
                wsplit_free[par] = T.tok()
            tmm = T.tok()
            for t in range(NT):
                V.wait(tmm)
                xsl = xs[:, D * t + 512 * ob: D * t + 512 * (ob + 1)]
                bank_free[t] = V.inc(nc.vector.tensor_tensor(
                    xsl, xsl, pb[t][0:128, 0:512], ALU.add))

    def ffn_phase():
        wf = W["wf0"]
        V.inc(nc.vector.memset(accw[:], 0.0))
        for ic in range(DC):
            for half in range(2):
                par = (2 * ic + half) % 2
                SYw[par].wait(wsplit_free[par])
                td = SYw[par].inc(nc.sync.dma_start(
                    out=wf32[par][:, 0:1024],
                    in_=AP(wf.tensor, 128 * ic * D + 1024 * half,
                           [[2048, 128], [1, 1024]])))
                V.wait(td)
                V.inc(nc.vector.tensor_reduce(sm1[:], wf32[par][:, 0:1024], AX.X,
                                              ALU.add, apply_absolute_value=True))
                tv = V.inc(nc.vector.tensor_add(accw[:], accw[:], sm1[:]))
                wsplit_free[par] = tv
        acq(0)
        T.wait(V.tok())
        tb2 = T.inc(nc.tensor.matmul(pb[0][0:1, 0:1], accw[:], onescol[:],
                                     start=True, stop=True))
        V.wait(tb2)
        V.inc(nc.vector.tensor_scalar(alv[0:1, 1:2], pb[0][0:1, 0:1],
                                      1.0 / (D * D), 1e-5, ALU.mult, ALU.max))
        bank_free[0] = V.tok()
        acq(1)
        T.wait(V.tok())
        tb3 = T.inc(nc.tensor.matmul(pb[1][0:128, 0:1], ones1f[:], alv[0:1, 1:2],
                                     start=True, stop=True))
        V.wait(tb3)
        V.inc(nc.vector.tensor_copy(alb[:], pb[1][0:128, 0:1]))
        bank_free[1] = V.tok()
        V.inc(nc.vector.tensor_scalar(thrp[:], alb[:], 0.5, None, ALU.mult))
        V.inc(nc.vector.tensor_scalar(thrn[:], thrp[:], -1.0, None, ALU.mult))
        td = SY.inc(nc.sync.dma_start(out=fbT[0:1, 0:2048], in_=W["fb0"][0:1, 0:2048]))
        for cb in range(4):
            w = 2 + (cb % 2)
            acq(w)
            T.wait(td)
            tb4 = T.inc(nc.tensor.matmul(pb[w][0:128, 0:512], ones1f[:],
                                         fbT[0:1, 512 * cb:512 * (cb + 1)],
                                         start=True, stop=True))
            V.wait(tb4)
            bank_free[w] = V.inc(nc.vector.tensor_copy(
                fbT[:, 512 * cb:512 * (cb + 1)], pb[w][0:128, 0:512]))
        tok_ntr = None
        for i in range(NT):
            V.wait(tok_ntr)
            ln_tile(i)
            V.inc(nc.vector.tensor_reduce(sm1[:], xnb[:], AX.X, ALU.max,
                                          apply_absolute_value=True))
            V.inc(nc.vector.tensor_scalar(sm2[:], sm1[:], 1e-5, 1.0 / 127.0,
                                          ALU.max, ALU.mult))
            V.inc(nc.vector.reciprocal(sm3[:], sm2[:]))
            V.inc(nc.vector.tensor_tensor(salq[:, i:i + 1], sm2[:], alb[:], ALU.mult))
            for c2 in range(2):
                xsl = xnb[:, 1024 * c2:1024 * (c2 + 1)]
                V.inc(nc.vector.tensor_scalar(lad1[:], xsl, sm3[:, 0:1], 256.5,
                                              ALU.mult, ALU.add))
                V.inc(nc.vector.tensor_copy(lad2[:], lad1[:]))
                for b in [256.0, 128.0, 64.0, 32.0, 16.0, 8.0, 4.0, 2.0, 1.0]:
                    V.inc(nc.vector.tensor_scalar(lad3[:], lad2[:], b, None, ALU.is_ge))
                    V.inc(nc.vector.scalar_tensor_tensor(lad2[:], lad3[:], -b,
                                                         lad2[:], ALU.mult, ALU.add))
                V.inc(nc.vector.tensor_sub(lad1[:], lad1[:], lad2[:]))
                V.inc(nc.vector.tensor_scalar(nbq[:, 1024 * c2:1024 * (c2 + 1)],
                                              lad1[:], 256.0, None, ALU.subtract))
            tq = V.tok()
            V.inc(nc.vector.tensor_add(xs[:, D * i:D * (i + 1)],
                                       xs[:, D * i:D * (i + 1)], fbT[:]))
            for cc in range(DC):
                w = cc % 2
                acq(w)
                T.wait(tq)
                tt = T.inc(nc.tensor.transpose(pb[w][0:128, 0:128],
                                               nbq[:, 128 * cc:128 * (cc + 1)],
                                               ident32[:]))
                S.wait(tt)
                th = S.inc(nc.scalar.copy(
                    nTq[:, 1024 * cc + 128 * i: 1024 * cc + 128 * (i + 1)],
                    pb[w][0:128, 0:128]))
                bank_free[w] = th
            tok_ntr = T.tok()
        join()
        tt_free = [None, None]
        for ob in range(4):
            for ic in range(DC):
                par = ic % 2
                SYw[par].wait(wsplit_free[par])
                td = SYw[par].inc(nc.sync.dma_start(
                    out=wf32[par][:, 0:512],
                    in_=AP(wf.tensor, 128 * ic * D + 512 * ob,
                           [[2048, 128], [1, 512]])))
                V.waits([td, tt_free[par]])
                V.inc(nc.vector.tensor_scalar(ft1[:], wf32[par][:, 0:512],
                                              thrp[:, 0:1], None, ALU.is_ge))
                V.inc(nc.vector.tensor_scalar(ft2[:], wf32[par][:, 0:512],
                                              thrn[:, 0:1], None, ALU.is_le))
                tv = V.inc(nc.vector.tensor_sub(ftt[par][:], ft1[:], ft2[:]))
                wsplit_free[par] = tv
                T.wait(tv)
                for t in range(NT):
                    if ic == 0:
                        acq(t)
                    T.inc(nc.tensor.matmul(
                        pb[t][0:128, 0:512],
                        nTq[:, 1024 * ic + 128 * t: 1024 * ic + 128 * (t + 1)],
                        ftt[par][:], start=(ic == 0), stop=(ic == DC - 1)))
                tt_free[par] = T.tok()
            tmm = T.tok()
            for t in range(NT):
                V.wait(tmm)
                xsl = xs[:, D * t + 512 * ob: D * t + 512 * (ob + 1)]
                bank_free[t] = V.inc(nc.vector.scalar_tensor_tensor(
                    xsl, pb[t][0:128, 0:512], salq[:, t:t + 1], xsl,
                    ALU.mult, ALU.add))

    def moe_phase():
        for t in range(NT):
            w = t % 2
            acq(w)
            for ic in range(DC):
                mm3(pb[w][0:128, 0:8],
                    xnTh[:, 1024 * ic + 128 * t: 1024 * ic + 128 * (t + 1)],
                    xnTl[:, 1024 * ic + 128 * t: 1024 * ic + 128 * (t + 1)],
                    gwh[:, 8 * ic:8 * (ic + 1)], gwl[:, 8 * ic:8 * (ic + 1)],
                    ic == 0, ic == DC - 1)
            tg = T.tok()
            V.wait(tg)
            V.inc(nc.vector.tensor_reduce(sm1[:], pb[w][0:128, 0:8], AX.X, ALU.max))
            tv = V.inc(nc.vector.tensor_scalar(sm1[:], sm1[:], -1.0, None, ALU.mult))
            S.wait(tv)
            tE = S.inc(nc.scalar.activation(g8a[:, 0:8], pb[w][0:128, 0:8], ACT.Exp,
                                            bias=sm1[:, 0:1]))
            V.wait(tE)
            bank_free[w] = tE
            V.inc(nc.vector.tensor_reduce(sm2[:], g8a[:, 0:8], AX.X, ALU.add))
            V.inc(nc.vector.reciprocal(sm2[:], sm2[:]))
            V.inc(nc.vector.tensor_scalar(g8a[:, 0:8], g8a[:, 0:8], sm2[:, 0:1],
                                          None, ALU.mult))
            V.inc(nc.vector.tensor_tensor(g8b[:, 0:8], g8a[:, 0:8], gbB[:, 0:8],
                                          ALU.add))
            V.inc(nc.vector.tensor_reduce(sm1[:], g8b[:, 0:8], AX.X, ALU.max))
            V.inc(nc.vector.tensor_scalar(dwbuf[:, 8 * t:8 * t + 8], g8b[:, 0:8],
                                          sm1[:, 0:1], None, ALU.is_equal))
            V.inc(nc.vector.scalar_tensor_tensor(
                g8c[:, 0:8], dwbuf[:, 8 * t:8 * t + 8], -1e30, g8b[:, 0:8],
                ALU.mult, ALU.add))
            V.inc(nc.vector.tensor_reduce(sm2[:], g8c[:, 0:8], AX.X, ALU.max))
            V.inc(nc.vector.tensor_scalar(g8b[:, 0:8], g8b[:, 0:8], sm2[:, 0:1],
                                          None, ALU.is_ge))
            V.inc(nc.vector.tensor_tensor(dwbuf[:, 8 * t:8 * t + 8], g8a[:, 0:8],
                                          g8b[:, 0:8], ALU.mult))
        join()
        mwsp = [None, None]
        hft_free = [None, None]

        def stream_m(base_off, wdram, row_len, par):
            SYw[par].wait(mwsp[par])
            td = SYw[par].inc(nc.sync.dma_start(
                out=wf32[par][:, 0:512],
                in_=AP(wdram.tensor, base_off, [[row_len, 128], [1, 512]])))
            S.wait(td)
            th = S.inc(nc.scalar.copy(wbh[par][:, 0:512], wf32[par][:, 0:512]))
            V.wait(th)
            return V.inc(nc.vector.scalar_tensor_tensor(
                wbl[par][:, 0:512], wf32[par][:, 0:512], 1.0,
                wbh[par][:, 0:512], ALU.mult, ALU.subtract))

        for e in range(9):
            shared = e == 8
            for ih in range(2):
                for tg in range(2):
                    for which, wd in [(0, W["w1"]), (1, W["w3"])]:
                        for ic in range(DC):
                            par = ic % 2
                            tw = stream_m(e * D * 1024 + 128 * ic * 1024 + 512 * ih,
                                          wd, 1024, par)
                            T.wait(tw)
                            for tl_ in range(4):
                                t = 4 * tg + tl_
                                if ic == 0:
                                    acq(tl_)
                                mm3(pb[tl_][0:128, 0:512],
                                    xnTh[:, 1024 * ic + 128 * t: 1024 * ic + 128 * (t + 1)],
                                    xnTl[:, 1024 * ic + 128 * t: 1024 * ic + 128 * (t + 1)],
                                    wbh[par][:, 0:512], wbl[par][:, 0:512],
                                    ic == 0, ic == DC - 1)
                            mwsp[par] = T.tok()
                        tmm = T.tok()
                        if which == 0:
                            for tl_ in range(4):
                                S.wait(tmm)
                                bank_free[tl_] = S.inc(nc.scalar.activation(
                                    s1b[:, 512 * tl_:512 * (tl_ + 1)],
                                    pb[tl_][0:128, 0:512], ACT.Silu))
                        else:
                            for tl_ in range(4):
                                t = 4 * tg + tl_
                                hp = tl_ % 2
                                V.waits([tmm, hft_free[hp]])
                                if shared:
                                    tF = V.inc(nc.vector.tensor_tensor(
                                        hFt[hp][:], pb[tl_][0:128, 0:512],
                                        s1b[:, 512 * tl_:512 * (tl_ + 1)], ALU.mult))
                                else:
                                    tF = V.inc(nc.vector.scalar_tensor_tensor(
                                        hFt[hp][:], pb[tl_][0:128, 0:512],
                                        dwbuf[:, 8 * t + e:8 * t + e + 1],
                                        s1b[:, 512 * tl_:512 * (tl_ + 1)],
                                        ALU.mult, ALU.mult))
                                bank_free[tl_] = tF
                                for cc in range(4):
                                    w = 4 + (cc % 2)
                                    acq(w)
                                    T.wait(tF)
                                    tt = T.inc(nc.tensor.transpose(
                                        pb[w][0:128, 0:128],
                                        hFt[hp][:, 128 * cc:128 * (cc + 1)],
                                        ident32[:]))
                                    dh_ = hTh[:, 1024 * cc + 128 * t: 1024 * cc + 128 * (t + 1)]
                                    dl_ = hTl[:, 1024 * cc + 128 * t: 1024 * cc + 128 * (t + 1)]
                                    S.wait(tt)
                                    th2 = S.inc(nc.scalar.copy(dh_, pb[w][0:128, 0:128]))
                                    V.wait(th2)
                                    tl2 = V.inc(nc.vector.scalar_tensor_tensor(
                                        dl_, pb[w][0:128, 0:128], 1.0, dh_,
                                        ALU.mult, ALU.subtract))
                                    bank_free[w] = tl2
                                hft_free[hp] = T.tok()
                tok_hT = V.tok()
                for db in range(4):
                    for icl in range(4):
                        ici = 4 * ih + icl
                        par = icl % 2
                        tw = stream_m(e * 1024 * D + 128 * ici * D + 512 * db,
                                      W["w2"], D, par)
                        T.waits([tw, tok_hT])
                        for t in range(NT):
                            if icl == 0:
                                acq(t)
                            mm3(pb[t][0:128, 0:512],
                                hTh[:, 1024 * icl + 128 * t: 1024 * icl + 128 * (t + 1)],
                                hTl[:, 1024 * icl + 128 * t: 1024 * icl + 128 * (t + 1)],
                                wbh[par][:, 0:512], wbl[par][:, 0:512],
                                icl == 0, icl == 3)
                        mwsp[par] = T.tok()
                    tmm = T.tok()
                    for t in range(NT):
                        V.wait(tmm)
                        xsl = xs[:, D * t + 512 * db: D * t + 512 * (db + 1)]
                        bank_free[t] = V.inc(nc.vector.tensor_tensor(
                            xsl, xsl, pb[t][0:128, 0:512], ALU.add))

    def run_layer(L, lim=99):
        lstage = min(stage, lim) if L == 0 else lim
        if L == 0:
            wq, wk, wv = W["wq0"], W["wk0"], W["wv0"]
            bq = bk = bv = None
            kvcols, nheads, dh, qh_per_kv = 2048, 16, 128, 1
            kt_rows, v_base = 4096, 8192
        else:
            wq, wk, wv = W["wq1"], W["wk1"], W["wv1"]
            bq, bk, bv = W["bq1"], W["bk1"], W["bv1"]
            kvcols, nheads, dh, qh_per_kv = 512, 8, 64, 4
            kt_rows, v_base = 1024, 2048

        rope_tables(L)
        join()
        for i in range(NT):
            tln = ln_tile(i)
            tsplit(xnb, 0, i, xnTh, xnTl, 0, DC, tln, banks=[0, 1])
        join()
        for i in range(NT):
            SY.inc(nc.sync.dma_start(out=xpark[128 * i:128 * (i + 1), :],
                                     in_=xs[:, D * i:D * (i + 1)]))
        join()
        if lstage < 2:
            return
        proj(L, wk, kvcols, bk, L, make_k_sink(L, kt_rows))
        vstg_free[0] = GD.tok()
        vstg_free[1] = GD.tok()
        proj(L, wv, kvcols, bv, 2, make_v_sink(L, v_base, nheads, dh))
        join()
        G.wait(GD.tok())
        CC.inc(nc.gpsimd.collective_compute(
            "AllGather", ALU.bypass,
            replica_groups=[[0, 1, 2, 3], [4, 5, 6, 7]],
            ins=[agin[L].opt()], outs=[agout[L].opt()]))
        if lstage < 3:
            return
        proj(L, wq, 2048, bq, L, q_sink)
        join()
        if lstage < 4:
            return
        attention(L, nheads, dh, kt_rows, v_base, qh_per_kv)
        join()
        wo_phase(L)
        join()
        if lstage < 5:
            return
        if L == 0:
            ffn_phase()
        else:
            for i in range(NT):
                tln = ln_tile(i)
                tsplit(xnb, 0, i, xnTh, xnTl, 0, DC, tln, banks=[0, 1])
            join()
            moe_phase()
        join()

    run_layer(0, lim=min(stage, 5))
    if stage >= 6:
        run_layer(1, lim=(4 if stage == 6 else 99))
    join()

    for i in range(NT):
        SY.inc(nc.sync.dma_start(out=out[128 * i:128 * (i + 1), :],
                                 in_=xs[:, D * i:D * (i + 1)]))
    nc.sync.wait_ge(sdSY, SY.cnt)

    blk_cm.__exit__(None, None, None)
    for cm in reversed(sem_cms):
        cm.__exit__(None, None, None)
    for cm in reversed(pb_cms):
        cm.__exit__(None, None, None)
    arena_cm.__exit__(None, None, None)
    return nc


_NC = None


def kernel(**inputs):
    global _NC
    if _NC is None:
        _NC = build_program()
    nc = _NC
    f32 = np.float32

    def tr(a):
        return np.ascontiguousarray(np.asarray(a, f32).T)

    wmap = dict(
        wq0=tr(inputs["L0_attn_Wq"]), wk0=tr(inputs["L0_attn_Wk"]),
        wv0=tr(inputs["L0_attn_Wv"]), wo0=tr(inputs["L0_attn_Wo"]),
        wf0=tr(inputs["L0_ffn_weight"]),
        fb0=np.asarray(inputs["L0_ffn_bias"], f32).reshape(1, D),
        wq1=tr(inputs["L1_attn_Wq"]),
        bq1=np.asarray(inputs["L1_attn_bq"], f32).reshape(1, D),
        wk1=tr(inputs["L1_attn_Wk"]),
        bk1=np.asarray(inputs["L1_attn_bk"], f32).reshape(1, 512),
        wv1=tr(inputs["L1_attn_Wv"]),
        bv1=np.asarray(inputs["L1_attn_bv"], f32).reshape(1, 512),
        wo1=tr(inputs["L1_attn_Wo"]),
        bo1=np.asarray(inputs["L1_attn_bo"], f32).reshape(1, D),
        gw=tr(inputs["L1_ffn_gate_w"]),
        gb=np.asarray(inputs["L1_ffn_gate_b"], f32).reshape(1, 8),
        w1=np.ascontiguousarray(np.concatenate(
            [np.asarray(inputs["L1_ffn_W1"], f32).transpose(0, 2, 1),
             np.asarray(inputs["L1_ffn_Ws1"], f32).T[None]], axis=0)),
        w2=np.ascontiguousarray(np.concatenate(
            [np.asarray(inputs["L1_ffn_W2"], f32).transpose(0, 2, 1),
             np.asarray(inputs["L1_ffn_Ws2"], f32).T[None]], axis=0)),
        w3=np.ascontiguousarray(np.concatenate(
            [np.asarray(inputs["L1_ffn_W3"], f32).transpose(0, 2, 1),
             np.asarray(inputs["L1_ffn_Ws3"], f32).T[None]], axis=0)),
    )
    x = np.asarray(inputs["L0_attn_x"], f32)
    in_maps = []
    for c in range(8):
        b, r = c // 4, c % 4
        xo = np.ascontiguousarray(x[b].reshape(32, 128, D)[r::4].reshape(TOK, D))
        m = dict(wmap)
        m["x"] = xo
        m["rk"] = np.array([[r]], f32)
        in_maps.append(m)

    res = run_bass_kernel_spmd(nc, in_maps, core_ids=list(range(8)))
    y = np.empty((2, 4096, D), f32)
    for c in range(8):
        b, r = c // 4, c % 4
        y[b].reshape(32, 128, D)[r::4] = res.results[c]["out"].reshape(NT, 128, D)
    return y
